# version 22
# baseline (speedup 1.0000x reference)
import math
from contextlib import ExitStack

import numpy as np
import ml_dtypes
import concourse.bass as bass
import concourse.mybir as mybir
from concourse.bass_utils import run_bass_kernel_spmd

F32 = mybir.dt.float32
BF16 = mybir.dt.bfloat16
I32 = mybir.dt.int32
AF = mybir.ActivationFunctionType
ALU = mybir.AluOpType
AX = mybir.AxisListType

H = 4096
S = 4096
T = 2048
NH = 32
QR = 1536
KVR = 512
RD = 64
ND = 128
VD = 128
QK = ND + RD
SC = QK ** -0.5
INTER = 11008
NPAIR = INTER // 128
GU_COLS = 2 * INTER
PK_GU = GU_COLS // 8
EPS_RMS = 1e-6
EPS_LN = 1e-5

A_TT = 256
QT = 512
N_QT = T // QT
C2P = 1024
DQW = 1024
ENGS = ("sy", "gp", "ve", "sc", "te")

_DEBUG_TAPS = ()
_SBUF_DEBUG = False
_PHASES = ("p0", "a1", "a2", "b", "b3", "c2", "c3")


class Sem:
    def __init__(self, es, nc, name):
        self.h = es.enter_context(nc.semaphore(name))
        self.n = 0


class Prog:

    def __init__(self):
        self.q = {e: [] for e in ENGS}

    def do(self, eng, fn, sem=None, k=1, fence=None):
        if fence is not None:
            self.q[eng].append(lambda e, fn=fn: fn(e))
            zb, zr = fence_bufs
            ap = fence
            self.q[eng].append(lambda e, ap=ap: None)
            self.q[eng].pop()
            if sem is not None:
                sem.n += k
                self.q[eng].append(
                    lambda e, ap=ap, s=sem, kk=k: e.matmul(
                        ap, zb[:], zr[:], start=False, stop=False,
                        skip_group_check=True).then_inc(s.h, kk))
            else:
                self.q[eng].append(
                    lambda e, ap=ap: e.matmul(ap, zb[:], zr[:], start=False,
                                              stop=False, skip_group_check=True))
        elif sem is not None:
            sem.n += k
            self.q[eng].append(lambda e, fn=fn, s=sem, kk=k: fn(e).then_inc(s.h, kk))
        else:
            self.q[eng].append(lambda e, fn=fn: fn(e))

    def dma(self, eng, out, in_, sem):
        self.do(eng, lambda e, o=out, i=in_: e.dma_start(out=o, in_=i), sem, 16)

    def wait(self, eng, sem, val=None):
        v = sem.n if val is None else val
        if v > 0:
            self.q[eng].append(lambda e, s=sem, v=v: e.wait_ge(s.h, v))

    def replay(self, nc):
        with nc.Block() as blk:
            @blk.sync
            def _(e):
                for fn in self.q["sy"]:
                    fn(e)

            @blk.gpsimd
            def _(e):
                for fn in self.q["gp"]:
                    fn(e)

            @blk.vector
            def _(e):
                for fn in self.q["ve"]:
                    fn(e)

            @blk.scalar
            def _(e):
                for fn in self.q["sc"]:
                    fn(e)

            @blk.tensor
            def _(e):
                for fn in self.q["te"]:
                    fn(e)


def build_program():
    nc = bass.Bass("TRN2", target_bir_lowering=False, debug=False)

    def din(name, shape, dtype=F32):
        return nc.dram_tensor(name, shape, dtype, kind="ExternalInput").ap()

    x_seq = din("x_seq", [S, H])
    cosF = din("cosF", [RD, S])
    sinF = din("sinF", [RD, S])
    cosO = din("cosO", [RD, T])
    sinO = din("sinO", [RD, T])
    wqa_l = din("wqa_l", [2, 128, 32, 768])
    g_qa = din("g_qa", [128, 12])
    wqb_n = din("wqb_n", [NH, 128, 12, 128])
    wqb_r = din("wqb_r", [NH, 128, 12, RD])
    w_kva = din("w_kva", [128, 32, 576])
    g_kva = din("g_kva", [128, 4])
    wkvb_k = din("wkvb_k", [NH, 128, 4, 128])
    wkvb_v = din("wkvb_v", [NH, 128, 4, 128])
    w_o = din("w_o", [H, H])
    qw_gu = din("qw_gu", [H, PK_GU], I32)
    qz_gu = din("qz_gu", [32, PK_GU], I32)
    sc_gu = din("sc_gu", [32, GU_COLS])
    qw_dn = din("qw_dn", [INTER, 512], I32)
    qz_dn = din("qz_dn", [NPAIR, 512], I32)
    sc_dn = din("sc_dn", [NPAIR, H])
    ident = din("ident", [128, 128], BF16)
    oneh32 = din("oneh32", [128, 32, 32], BF16)
    oneh86 = din("oneh86", [128, NPAIR, NPAIR], BF16)

    y = nc.dram_tensor("y", [T, H], F32, kind="ExternalOutput").ap()

    def scr(name, shape, dtype):
        kind = "ExternalOutput" if name in _DEBUG_TAPS else "Internal"
        return nc.dram_tensor(name, shape, dtype, kind=kind).ap()

    zsgu_d = scr("zsgu_d", [32, GU_COLS], F32)
    zsdn_d = scr("zsdn_d", [NPAIR, H], F32)
    wgu_d = scr("wgu_d", [32, 128, GU_COLS], BF16)
    wdn_d = scr("wdn_d", [NPAIR, 128, H], BF16)
    h_d = scr("h_d", [2, NPAIR, 128, C2P], BF16)
    osc_d = scr("osc_d", [NH, N_QT, 128, QT], BF16)
    x2_d = scr("x2_d", [T, H], F32)

    es = ExitStack()
    with es:
        SN = ["ld", "lds", "st", "pe", "act", "dve", "aux", "exp", "dqv", "dqa"]
        sm = {k: Sem(es, nc, k) for k in SN}
        ld, lds, st = sm["ld"], sm["lds"], sm["st"]
        pe, act, dve, aux = sm["pe"], sm["act"], sm["dve"], sm["aux"]
        s_exp = sm["exp"]
        dqv, dqa = sm["dqv"], sm["dqa"]

        ps = [es.enter_context(nc.psum_tensor(f"ps{i}", [128, 512], F32))
              for i in range(6)]
        pstA = es.enter_context(nc.psum_tensor("pstA", [128, 1024], BF16))
        pstB = es.enter_context(nc.psum_tensor("pstB", [128, 1024], BF16))
        psf = [pstA[:].bitcast(F32), pstB[:].bitcast(F32)]

        sb = lambda name, shape, dtype: es.enter_context(nc.sbuf_tensor(name, shape, dtype))
        ones_f = sb("ones_f", [128, 1], F32)
        onesr_f = sb("onesr_f", [1, 128], F32)
        ones64_f = sb("ones64_f", [64, 1], F32)
        ones_bf = sb("ones_bf", [128, 1], BF16)
        ones64_bf = sb("ones64_bf", [64, 1], BF16)
        id_bf = sb("id_bf", [128, 128], BF16)
        gqa_sb = sb("gqa_sb", [128, 12], F32)
        c_epsln = sb("c_epsln", [128, 1], F32)
        c_tiny = sb("c_tiny", [128, 1], F32)
        gkva_sb = sb("gkva_sb", [128, 4], F32)
        zero_bf = sb("zero_bf", [128, 1], BF16)
        global fence_bufs
        fence_bufs = (zero_bf, id_bf)
        esAB = ExitStack()
        kcnT = esAB.enter_context(nc.sbuf_tensor("kcnT", [128, 4, S], BF16))
        kroT = esAB.enter_context(nc.sbuf_tensor("kroT", [RD + 1, S], BF16))
        kmr = esAB.enter_context(nc.sbuf_tensor("kmr", [1, 8], F32))
        qcgT = esAB.enter_context(nc.sbuf_tensor("qcgT", [128, 12, T], BF16))
        rbq = esAB.enter_context(nc.sbuf_tensor("rbq", [128, T], F32))

        def emit_ln(P, x_ap, sqv_ap, rs_ap, out_ap, n):
            P.do("ve", lambda v: v.reduce_sum(out=rs_ap[:, 0:1], in_=x_ap, axis=AX.X), dve)
            P.wait("sc", dve)
            P.do("sc", lambda s: s.activation(out=sqv_ap, in_=x_ap, func=AF.Square), act)
            P.wait("ve", act)
            P.do("ve", lambda v: v.reduce_sum(out=rs_ap[:, 1:2], in_=sqv_ap, axis=AX.X),
                 dve)
            P.do("ve", lambda v: v.tensor_scalar_mul(out=rs_ap[:, 2:3], in0=rs_ap[:, 0:1],
                                                     scalar1=1.0 / n), dve)
            P.wait("ve", dve)
            P.do("ve", lambda v: v.tensor_mul(out=rs_ap[:, 3:4], in0=rs_ap[:, 2:3],
                                              in1=rs_ap[:, 2:3]), dve)
            P.do("ve", lambda v: v.tensor_scalar_mul(out=rs_ap[:, 4:5], in0=rs_ap[:, 1:2],
                                                     scalar1=1.0 / n), dve)
            P.wait("ve", dve)
            P.do("ve", lambda v: v.tensor_sub(out=rs_ap[:, 5:6], in0=rs_ap[:, 4:5],
                                              in1=rs_ap[:, 3:4]), dve)
            P.wait("sc", dve)
            P.do("sc", lambda s: s.activation(out=rs_ap[:, 6:7], in_=rs_ap[:, 5:6],
                                              func=AF.Ln, bias=c_epsln[:]), act)
            P.wait("sc", act)
            P.do("sc", lambda s: s.activation(out=rs_ap[:, 7:8],
                                              in_=rs_ap[:, 6:7], func=AF.Exp,
                                              scale=-0.5), act)
            P.wait("ve", act)
            P.do("ve", lambda v: v.tensor_mul(out=rs_ap[:, 3:4], in0=rs_ap[:, 2:3],
                                              in1=rs_ap[:, 7:8]), dve)
            P.wait("ve", dve)
            P.do("ve", lambda v: v.tensor_scalar_mul(out=rs_ap[:, 4:5], in0=rs_ap[:, 3:4],
                                                     scalar1=-1.0), dve)
            P.wait("ve", dve)
            P.do("ve", lambda v: v.tensor_scalar(out=out_ap, in0=x_ap,
                                                 scalar1=rs_ap[:, 7:8],
                                                 scalar2=rs_ap[:, 4:5],
                                                 op0=ALU.mult, op1=ALU.add), act)

        def dmy(P, ap):
            P.do("te", lambda t, ap=ap: t.matmul(ap, zero_bf[:], zero_bf[:],
                                                 start=False, stop=False,
                                                 skip_group_check=True))

        def emit_transposes(P, lnb_ap, lnT_ap, tok0):
            for grp in range(4):
                pb = pstA if grp % 2 == 0 else pstB
                P.wait("te", dve)
                P.wait("te", act)
                for kk in range(8):
                    k = grp * 8 + kk
                    P.do("te", lambda t, k=k, kk=kk, pb=pb: t.transpose(
                        pb[:, kk * 128:(kk + 1) * 128],
                        lnb_ap[:, k * 128:(k + 1) * 128], id_bf[:]),
                        pe if kk == 7 else None)
                P.wait("ve", pe)
                P.do("ve", lambda v, grp=grp, pb=pb: v.tensor_copy(
                    out=lnT_ap[:, grp * 8:(grp + 1) * 8, tok0:tok0 + 128],
                    in_=pb[:].rearrange("p (k c) -> p k c", c=128)), dve)

        if "p0" in _PHASES:
            P = Prog()
            with ExitStack() as ps0:
                lb = lambda n, sh, dt: ps0.enter_context(nc.sbuf_tensor(n, sh, dt))
                zpk = lb("zpk", [128, 688], I32)
                zpl = lb("zpl", [128, 5504], I32)
                zff = lb("zff", [128, 5504], F32)
                scs = lb("scs", [128, 5504], F32)
                zsf = lb("zsf", [128, 5504], F32)
                zsb = lb("zsb", [128, 5504], F32)
                P.dma("sy", id_bf[:], ident, ld)
                P.dma("sy", gqa_sb[:], g_qa, ld)
                P.dma("sy", gkva_sb[:], g_kva, ld)
                for buf, val in ((ones_f, 1.0), (onesr_f, 1.0), (ones64_f, 1.0),
                                 (ones_bf, 1.0), (ones64_bf, 1.0), (zero_bf, 0.0),
                                 (c_epsln, EPS_LN), (c_tiny, 1e-30)):
                    P.do("ve", lambda v, b=buf, val=val: v.memset(b[:], val))
                P.do("ve", lambda v: v.memset(kroT[RD:RD + 1, :], 1.0))

                def zs_chunk(pk_ap, sc_ap, out_ap, npk, nde, gu):
                    P.wait("sy", dve)
                    P.dma("sy", zpk[0:32 if gu else NPAIR, 0:npk], pk_ap, ld)
                    P.dma("sy", scs[0:32 if gu else NPAIR, 0:nde], sc_ap, ld)
                    rows = slice(0, 32 if gu else NPAIR)
                    P.wait("ve", ld)
                    P.wait("ve", act)
                    for j in range(8):
                        if gu:
                            o = zpl[rows, 0:nde].rearrange("p (g j c) -> p g j c",
                                                           j=8, c=16)[:, :, j, :]
                            i = zpk[rows, 0:npk].rearrange("p (g c) -> p g c", c=16)
                        else:
                            o = zpl[rows, 0:nde].rearrange("p (c j) -> p c j",
                                                           j=8)[:, :, j]
                            i = zpk[rows, 0:npk]
                        P.do("ve", lambda v, o=o, i=i, j=j: v.tensor_scalar(
                            out=o, in0=i, scalar1=4 * j, scalar2=15,
                            op0=ALU.logical_shift_right, op1=ALU.bitwise_and),
                            dve if j == 7 else None)
                    P.wait("sc", dve)
                    P.wait("sc", aux)
                    P.do("sc", lambda s, rows=rows, nde=nde: s.activation(
                        out=zff[rows, 0:nde], in_=zpl[rows, 0:nde], func=AF.Copy), act)
                    P.wait("ve", act)
                    P.wait("ve", st)
                    P.do("ve", lambda v, rows=rows, nde=nde: v.tensor_scalar(
                        out=zsf[rows, 0:nde], in0=zff[rows, 0:nde],
                        scalar1=-1.0, scalar2=7.5,
                        op0=ALU.mult, op1=ALU.add), aux)
                    P.wait("ve", aux)
                    P.do("ve", lambda v, rows=rows, nde=nde: v.tensor_mul(
                        out=zsb[rows, 0:nde], in0=zsf[rows, 0:nde],
                        in1=scs[rows, 0:nde]), dve)
                    P.wait("gp", dve)
                    P.dma("gp", out_ap, zsb[rows, 0:nde], st)

                for ch in range(4):
                    zs_chunk(qz_gu[:, ch * 688:(ch + 1) * 688],
                             sc_gu[:, ch * 5504:(ch + 1) * 5504],
                             zsgu_d[:, ch * 5504:(ch + 1) * 5504], 688, 5504, True)
                zs_chunk(qz_dn, sc_dn, zsdn_d, 512, H, False)
                P.wait("gp", st)
                P.replay(nc)

        if "a1" in _PHASES:
            n_tt = S // A_TT
            P = Prog()
            with ExitStack() as psA:
                lb = lambda n, sh, dt: psA.enter_context(nc.sbuf_tensor(n, sh, dt))
                xa = lb("xa1", [128, 2, H], F32)
                sqv = lb("sqv1", [128, H], BF16)
                rsa = lb("rsa1", [128, 8], F32)
                lnb = lb("lnb1", [128, H], BF16)
                lnT = lb("lnT1", [128, 32, A_TT], BF16)
                wkva_bf = lb("wkva_bf", [128, 2, 576], BF16)
                cosF_sb = lb("cosF_sb", [RD, A_TT], F32)
                sinF_sb = lb("sinF_sb", [RD, A_TT], F32)
                kvsq = lb("kvsq", [128, A_TT], F32)
                r1s = lb("r1s", [1, 3 * A_TT], F32)
                rbk = lb("rbk", [128, A_TT], F32)
                ktmp = lb("ktmp", [64, A_TT], F32)
                ktmp2 = lb("ktmp2", [64, A_TT], F32)
                kha = lb("kha", [32, A_TT], F32)
                khb = lb("khb", [32, A_TT], F32)
                kev = lb("kev", [128, A_TT], F32)
                kv_dst = (ps[0][:, 0:256], ps[1][:, 0:256], ps[2][:, 0:256],
                          ps[3][:, 0:256], ps[4][0:64, 0:256])
                wk_free = [0, 0]
                cos_free = 0
                evk = {}
                for tt in range(n_tt):
                    sl = slice(tt * A_TT, (tt + 1) * A_TT)
                    P.wait("sy", act, act.n)
                    for hh in range(2):
                        P.dma("sy", xa[:, hh, :],
                              x_seq[tt * A_TT + hh * 128: tt * A_TT + (hh + 1) * 128, :],
                              ld)
                    P.wait("sy", dve, cos_free)
                    P.dma("sy", cosF_sb[:], cosF[:, sl], ld)
                    P.dma("sy", sinF_sb[:], sinF[:, sl], ld)
                    for hh in range(2):
                        P.wait("ve", ld)
                        P.wait("sc", ld)
                        emit_ln(P, xa[:, hh, :], sqv[:], rsa[:], lnb[:], H)
                        P.wait("te", act)
                        emit_transposes(P, lnb[:], lnT[:], hh * 128)
                    if tt >= 1:
                        P.wait("te", act, evk[tt - 1][1])
                    P.wait("te", dve)
                    for k in range(32):
                        bi = k % 2
                        P.wait("gp", pe, wk_free[bi])
                        P.dma("gp", wkva_bf[:, bi, :], w_kva[:, k, :], lds)
                        P.wait("te", lds)
                        dmy(P, kv_dst[0][0:1, 0:1])
                        for m in range(5):
                            mw = 128 if m < 4 else 64
                            P.do("te", lambda t, m=m, k=k, mw=mw, bi=bi: t.matmul(
                                kv_dst[m], wkva_bf[:, bi, m * 128:m * 128 + mw],
                                lnT[:, k, :], start=(k == 0), stop=(k == 31)),
                                pe if m == 4 else None,
                                fence=(kv_dst[4][0:1, 0:128] if (m == 4 and k == 31)
                                       else None))
                        wk_free[bi] = pe.n
                    P.wait("sc", pe)
                    for m in range(4):
                        if m > 0:
                            P.wait("sc", aux)
                        P.do("sc", lambda s, m=m: s.activation(
                            out=kvsq[:], in_=kv_dst[m], func=AF.Square), act)
                        P.wait("te", act)
                        P.do("te", lambda t, m=m: t.matmul(
                            ps[5][0:1, 0:256], ones_f[:], kvsq[:],
                            start=(m == 0), stop=(m == 3)), aux,
                            fence=(ps[5][0:1, 0:128] if m == 3 else None))
                    P.wait("sc", aux)
                    P.do("sc", lambda s: s.activation(out=r1s[0:1, 0:256],
                                                      in_=ps[5][0:1, 0:256],
                                                      func=AF.Ln, scale=1.0 / KVR), act)
                    P.wait("sc", act)
                    P.do("sc", lambda s: s.activation(out=r1s[0:1, 256:512],
                                                      in_=r1s[0:1, 0:256],
                                                      func=AF.Exp, scale=0.5), act)
                    P.wait("ve", act)
                    P.do("ve", lambda v: v.tensor_scalar_add(
                        out=r1s[0:1, 512:768], in0=r1s[0:1, 256:512],
                        scalar1=EPS_RMS), dve)
                    P.wait("ve", dve)
                    P.do("ve", lambda v: v.reciprocal(
                        out=r1s[0:1, 0:256], in_=r1s[0:1, 512:768]), dve)
                    P.wait("te", dve)
                    P.do("te", lambda t: t.matmul(psf[1][:, 0:256], onesr_f[:],
                                                  r1s[0:1, 0:256],
                                                  start=True, stop=True), pe,
                         fence=psf[1][0:1, 0:128])
                    P.wait("sc", pe)
                    P.do("sc", lambda s: s.activation(out=rbk[:], in_=psf[1][:, 0:256],
                                                      func=AF.Copy), act)
                    for m in range(4):
                        P.wait("ve", act)
                        P.do("ve", lambda v, m=m: v.tensor_mul(
                            out=kev[:], in0=kv_dst[m], in1=rbk[:]), dve)
                        P.wait("sc", dve)
                        P.do("sc", lambda s, m=m, sl=sl: s.activation(
                            out=kcnT[:, m, sl], in_=kev[:], func=AF.Copy,
                            scale=gkva_sb[:, m:m + 1]), act)
                    P.do("ve", lambda v: v.tensor_mul(
                        out=kha[:], in0=ps[4][0:32, 0:256], in1=cosF_sb[0:32, :]))
                    P.do("ve", lambda v: v.tensor_mul(
                        out=khb[:], in0=ps[4][32:64, 0:256], in1=sinF_sb[0:32, :]), dve)
                    P.wait("ve", dve)
                    P.do("ve", lambda v: v.tensor_sub(
                        out=ktmp[0:32, :], in0=kha[:], in1=khb[:]))
                    P.do("ve", lambda v: v.tensor_mul(
                        out=kha[:], in0=ps[4][32:64, 0:256], in1=cosF_sb[32:64, :]))
                    P.do("ve", lambda v: v.tensor_mul(
                        out=khb[:], in0=ps[4][0:32, 0:256], in1=sinF_sb[32:64, :]), dve)
                    P.wait("ve", dve)
                    P.do("ve", lambda v: v.tensor_add(
                        out=ktmp[32:64, :], in0=kha[:], in1=khb[:]),
                        dve)
                    cos_free = dve.n
                    evk[tt] = (dve.n, act.n)
                    P.wait("sc", dve)
                    P.do("sc", lambda s, sl=sl: s.activation(
                        out=kroT[0:RD, sl], in_=ktmp[:], func=AF.Copy))
                    P.do("sc", lambda s: s.activation(out=ktmp2[:], in_=ktmp[:],
                                                      func=AF.Square), act)
                    P.wait("te", act)
                    P.do("te", lambda t: t.matmul(ps[5][0:1, 256:512], ones64_f[:],
                                                  ktmp2[:], start=True, stop=True), pe,
                         fence=ps[5][0:1, 256:384])
                    P.wait("ve", pe)
                    s_prev = (tt + 1) % 2
                    s_cur = tt % 2
                    P.do("ve", lambda v: v.reduce_max(out=kmr[0:1, 256:512] if False
                                                      else kmr[0:1, 2:3],
                                                      in_=ps[5][0:1, 256:512],
                                                      axis=AX.X), dve)
                    P.wait("ve", dve)
                    if tt == 0:
                        P.do("ve", lambda v: v.tensor_copy(out=kmr[0:1, 0:1],
                                                           in_=kmr[0:1, 2:3]), dve)
                    else:
                        P.do("ve", lambda v, a=s_prev, b=s_cur: v.tensor_max(
                            out=kmr[0:1, b:b + 1], in0=kmr[0:1, a:a + 1],
                            in1=kmr[0:1, 2:3]), dve)
                    P.wait("te", dve)
                fin = (n_tt - 1) % 2
                P.do("ve", lambda v, fin=fin: v.tensor_copy(
                    out=kmr[0:1, 3:4], in_=kmr[0:1, fin:fin + 1]), dve)
                if _SBUF_DEBUG:
                    for nm, tns in [("a1_lnT", lnT), ("a1_lnb", lnb), ("a1_sqv", sqv),
                                    ("a1_rsa", rsa), ("a1_xa", xa), ("a1_kcnT", kcnT),
                                    ("a1_kroT", kroT), ("a1_kmr", kmr),
                                    ("a1_rbk", rbk), ("a1_r1s", r1s)]:
                        dd = nc.dram_tensor("dbg_" + nm, list(tns[:].shape),
                                            tns[:].dtype, kind="ExternalOutput").ap()
                        P.wait("gp", dve)
                        P.wait("gp", act)
                        P.wait("gp", pe)
                        P.dma("gp", dd, tns[:], st)
                    P.wait("gp", st)
                P.replay(nc)

        if "a2" in _PHASES:
            P = Prog()
            with ExitStack() as psA2:
                lb = lambda n, sh, dt: psA2.enter_context(nc.sbuf_tensor(n, sh, dt))
                xa = lb("xa2", [128, H], F32)
                sqv = lb("sqv2", [128, H], BF16)
                rsa = lb("rsa2", [128, 8], F32)
                lnb = lb("lnb2", [128, H], BF16)
                lnT = lb("lnT2", [128, 32, 512], BF16)
                wqa_bf = lb("wqa_bf", [128, 2, 768], BF16)
                qsq = lb("qsq2", [128, 512], F32)
                ssq = lb("ssq", [1, T], F32)
                ssq2 = lb("ssq2", [1, T], F32)
                r1q = lb("r1q", [1, 1536], F32)
                n_ttg = T // 512
                wq_free = [0, 0]
                evq_prev = 0
                for mg in range(2):
                    for ttg in range(n_ttg):
                        for sub in range(4):
                            row = ttg * 512 + sub * 128
                            P.wait("sy", act, act.n)
                            P.dma("sy", xa[:], x_seq[row:row + 128, :], ld)
                            P.wait("ve", ld)
                            P.wait("sc", ld)
                            emit_ln(P, xa[:], sqv[:], rsa[:], lnb[:], H)
                            P.wait("te", act)
                            P.wait("te", aux)
                            emit_transposes(P, lnb[:], lnT[:], sub * 128)
                        if ttg >= 1 or mg >= 1:
                            P.wait("te", act, evq_prev)
                        P.wait("te", dve)
                        for k in range(32):
                            bi = k % 2
                            P.wait("gp", pe, wq_free[bi])
                            P.dma("gp", wqa_bf[:, bi, :], wqa_l[mg, :, k, :], lds)
                            P.wait("te", lds)
                            dmy(P, ps[0][0:1, 0:1])
                            for m in range(6):
                                P.do("te", lambda t, m=m, k=k, bi=bi: t.matmul(
                                    ps[m][:], wqa_bf[:, bi, m * 128:(m + 1) * 128],
                                    lnT[:, k, :], start=(k == 0), stop=(k == 31)),
                                    pe if m == 5 else None,
                                    fence=(ps[5][0:1, 0:128] if (m == 5 and k == 31)
                                           else None))
                            wq_free[bi] = pe.n
                        sl = slice(ttg * 512, (ttg + 1) * 512)
                        P.wait("sc", pe)
                        for m in range(6):
                            if m > 0:
                                P.wait("sc", aux)
                            P.do("sc", lambda s, m=m: s.activation(
                                out=qsq[:], in_=ps[m][:], func=AF.Square), act)
                            P.wait("te", act)
                            P.do("te", lambda t, m=m: t.matmul(
                                psf[0][0:1, :], ones_f[:], qsq[:],
                                start=(m == 0), stop=(m == 5)), aux,
                                fence=(psf[0][0:1, 0:128] if m == 5 else None))
                            P.do("sc", lambda s, m=m, mg=mg, sl=sl: s.activation(
                                out=qcgT[:, 6 * mg + m, sl], in_=ps[m][:], func=AF.Copy,
                                scale=gqa_sb[:, 6 * mg + m:6 * mg + m + 1]), act)
                        evq_prev = act.n
                        P.wait("ve", aux)
                        if mg == 0:
                            P.do("ve", lambda v, sl=sl: v.tensor_copy(
                                out=ssq[0:1, sl], in_=psf[0][0:1, :]), dve)
                        else:
                            P.do("ve", lambda v, sl=sl: v.tensor_add(
                                out=ssq2[0:1, sl], in0=psf[0][0:1, :],
                                in1=ssq[0:1, sl]), dve)
                        P.wait("te", dve)
                for qs in range(T // 512):
                    sl = slice(qs * 512, (qs + 1) * 512)
                    P.do("sc", lambda s, sl=sl: s.activation(
                        out=r1q[0:1, 0:512], in_=ssq2[0:1, sl], func=AF.Ln,
                        scale=1.0 / QR), act)
                    P.wait("sc", act)
                    P.do("sc", lambda s: s.activation(
                        out=r1q[0:1, 512:1024], in_=r1q[0:1, 0:512],
                        func=AF.Exp, scale=0.5), act)
                    P.wait("ve", act)
                    P.do("ve", lambda v: v.tensor_scalar_add(
                        out=r1q[0:1, 1024:1536], in0=r1q[0:1, 512:1024],
                        scalar1=EPS_RMS), dve)
                    P.wait("ve", dve)
                    P.do("ve", lambda v: v.reciprocal(
                        out=r1q[0:1, 0:512], in_=r1q[0:1, 1024:1536]), dve)
                    P.wait("te", dve)
                    P.wait("te", act)
                    P.do("te", lambda t: t.matmul(psf[1][:], onesr_f[:],
                                                  r1q[0:1, 0:512],
                                                  start=True, stop=True), pe,
                         fence=psf[1][0:1, 0:128])
                    P.wait("sc", pe)
                    P.do("sc", lambda s, sl=sl: s.activation(
                        out=rbq[:, sl], in_=psf[1][:], func=AF.Copy, scale=SC), act)
                P.replay(nc)

        if "b" in _PHASES:
            P = Prog()
            with ExitStack() as psB:
                lb = lambda n, sh, dt: psB.enter_context(nc.sbuf_tensor(n, sh, dt))
                wqbn_sb = lb("wqbn_sb", [128, 12, 128], BF16)
                wqbr_sb = lb("wqbr_sb", [128, 12, RD], BF16)
                wkbk_sb = lb("wkbk_sb", [128, 2, 4, 128], BF16)
                wkbv_sb = lb("wkbv_sb", [128, 2, 4, 128], BF16)
                KnT = lb("KnT", [128, S], BF16)
                Vkm = lb("Vkm", [128, 32, VD], BF16)
                qnT = lb("qnT", [128, T], BF16)
                qrA = lb("qrA", [RD + 1, T], BF16)
                qrf_a = lb("qrf_a", [32, 512], F32)
                qrf_b = lb("qrf_b", [32, 512], F32)
                qtmp = lb("qtmp", [64, 512], F32)
                qha = lb("qha", [32, 512], F32)
                qhb = lb("qhb", [32, 512], F32)
                cosO_sb = lb("cosO_sb", [RD, 512], F32)
                sinO_sb = lb("sinO_sb", [RD, 512], F32)
                probs = lb("probs", [128, 32, QT], BF16)
                qsq_b = lb("qsq_b", [128, QT], BF16)
                qsq_r = lb("qsq_r", [64, QT], BF16)
                k2sq = lb("k2sq", [128, 512], BF16)
                kmax8 = lb("kmax8", [1, 8], F32)
                km2 = lb("km2", [1, 4], F32)
                crow = lb("crow", [1, 512], BF16)
                rdrow = lb("rdrow", [1, 512], F32)
                rbden = lb("rbden", [128, 512], F32)
                dpart = lb("dpart", [128, 512], F32)
                obf = lb("obf", [128, QT], BF16)
                qpk = lb("qpk", [128, 3, 128], I32)
                wnib = lb("wnib", [128, DQW], I32)
                wbf = lb("wbf", [128, 2, DQW], BF16)
                sb_f = lb("sb_f", [128, 3, DQW], BF16)
                wden = lb("wden", [128, 2, DQW], BF16)

                dqs = {"u": 0, "dve3": [0, 0, 0], "cvfree": 0,
                       "wbffree": [0, 0], "wdfree": [0, 0], "pend": []}

                def dq_stageA(rec):
                    b3 = rec["u"] % 3
                    npk, nde = rec["npk"], rec["nde"]
                    P.wait("gp", dqv, dqs["dve3"][b3])
                    P.dma("gp", qpk[:, b3, 0:npk], rec["pk"], lds)
                    P.dma("gp", sb_f[:, b3, 0:nde],
                          rec["s"].partition_broadcast(128).squeeze(1), lds)
                    rec["sbm"] = lds.n
                    P.wait("ve", lds)
                    P.wait("ve", dqa, dqs["cvfree"])
                    for j in range(8):
                        if rec["gu"]:
                            o = wnib[:, 0:nde].rearrange(
                                "p (g j c) -> p g j c", j=8, c=16)[:, :, j, :]
                            i = qpk[:, b3, 0:npk].rearrange("p (g c) -> p g c", c=16)
                        else:
                            o = wnib[:, 0:nde].rearrange(
                                "p (c j) -> p c j", j=8)[:, :, j]
                            i = qpk[:, b3, 0:npk]
                        P.do("ve", lambda v, o=o, i=i, j=j: v.tensor_scalar(
                            out=o, in0=i, scalar1=4 * j, scalar2=15,
                            op0=ALU.logical_shift_right, op1=ALU.bitwise_and),
                            dqv if j == 7 else None)
                    rec["shd"] = dqv.n

                def dq_stageB(rec):
                    b2 = rec["u"] % 2
                    nde = rec["nde"]
                    P.wait("sc", dqv, rec["shd"])
                    P.wait("sc", dqv, dqs["wbffree"][b2])
                    P.do("sc", lambda s, b2=b2, nde=nde: s.activation(
                        out=wbf[:, b2, 0:nde], in_=wnib[:, 0:nde],
                        func=AF.Copy), dqa)
                    rec["cvd"] = dqa.n
                    dqs["cvfree"] = dqa.n

                def dq_stageC(rec):
                    b3 = rec["u"] % 3
                    b2 = rec["u"] % 2
                    nde = rec["nde"]
                    P.wait("ve", dqa, rec["cvd"])
                    P.wait("ve", lds, rec["sbm"])
                    P.wait("ve", st, dqs["wdfree"][b2])
                    P.do("ve", lambda v, b2=b2, b3=b3, nde=nde:
                         v.scalar_tensor_tensor(
                             out=wden[:, b2, 0:nde], in0=wbf[:, b2, 0:nde],
                             scalar=-7.5, in1=sb_f[:, b3, 0:nde],
                             op0=ALU.add, op1=ALU.mult), dqv)
                    rec["std"] = dqv.n
                    dqs["dve3"][b3] = dqv.n
                    dqs["wbffree"][b2] = dqv.n

                def dq_stageD(rec):
                    b2 = rec["u"] % 2
                    P.wait("sy", dqv, rec["std"])
                    P.dma("sy", rec["out"], wden[:, b2, 0:rec["nde"]], st)
                    dqs["wdfree"][b2] = st.n

                def dq_advance(rec):
                    pend = dqs["pend"]
                    pend.append(rec)
                    if len(pend) >= 4:
                        if pend[-4] is not None:
                            dq_stageD(pend[-4])
                    if len(pend) >= 3 and pend[-3] is not None:
                        dq_stageC(pend[-3])
                    if len(pend) >= 2 and pend[-2] is not None:
                        dq_stageB(pend[-2])
                    if rec is not None:
                        dq_stageA(rec)
                    if len(pend) >= 4:
                        pend.pop(0)

                def emit_dq(pk_ap, s_ap, out_ap, npk, nde, gu):
                    rec = {"u": dqs["u"], "pk": pk_ap, "s": s_ap, "out": out_ap,
                           "npk": npk, "nde": nde, "gu": gu}
                    dqs["u"] += 1
                    dq_advance(rec)

                def dq_flush():
                    for _ in range(3):
                        dq_advance(None)

                def emit_gu_unit(k, u):
                    ng = 8 if u < 21 else 4
                    nde = ng * 128
                    emit_dq(qw_gu[k * 128:(k + 1) * 128, u * 128:u * 128 + ng * 16],
                            sc_gu[k:k + 1, u * DQW:u * DQW + nde],
                            wgu_d[k, :, u * DQW:u * DQW + nde], ng * 16, nde, True)

                def emit_dn_unit(i, qu):
                    srcq = qw_dn[i * 128:(i + 1) * 128]
                    emit_dq(srcq[:, qu * 128:(qu + 1) * 128],
                            sc_dn[i:i + 1, qu * DQW:(qu + 1) * DQW],
                            wdn_d[i, :, qu * DQW:(qu + 1) * DQW], 128, DQW, False)

                dq_units = [("gu", k, u) for k in range(32) for u in range(22)] + \
                           [("dn", i, qu) for i in range(NPAIR) for qu in range(4)]
                n_dq = len(dq_units)
                dq_pos = 0

                def prefetch_kv_w(h):
                    bi = h % 2
                    P.wait("gp", pe, pe.n)
                    P.dma("gp", wkbk_sb[:, bi], wkvb_k[h], lds)
                    P.dma("gp", wkbv_sb[:, bi], wkvb_v[h], lds)
                    return lds.n

                lds_need = prefetch_kv_w(0)
                osc_guard = 0
                cos_free = 0
                den_guard_pe = 0
                den_guard_act = 0
                for h in range(NH):
                    bi = h % 2
                    P.wait("gp", pe, pe.n)
                    P.dma("gp", wqbn_sb[:], wqb_n[h], lds)
                    P.dma("gp", wqbr_sb[:], wqb_r[h], lds)
                    qw_need = lds.n
                    P.wait("te", lds, lds_need)
                    if h + 1 < NH:
                        lds_need = prefetch_kv_w(h + 1)
                    for sl8 in range(8):
                        pb = psf[sl8 % 2]
                        P.wait("te", act)
                        for c in range(4):
                            P.do("te", lambda t, pb=pb, c=c, sl8=sl8, bi=bi: t.matmul(
                                pb, wkbk_sb[:, bi, c, :],
                                kcnT[:, c, sl8 * 512:(sl8 + 1) * 512],
                                start=(c == 0), stop=(c == 3)), pe if c == 3 else None,
                                fence=(pb[0:1, 0:128] if c == 3 else None))
                        P.wait("sc", pe)
                        P.do("sc", lambda s, pb=pb, sl8=sl8: s.activation(
                            out=KnT[:, sl8 * 512:(sl8 + 1) * 512], in_=pb, func=AF.Copy))
                        P.do("sc", lambda s, pb=pb: s.activation(
                            out=k2sq[:], in_=pb, func=AF.Square), act)
                        P.wait("te", act)
                        P.wait("te", dve)
                        P.do("te", lambda t: t.matmul(ps[4][0:1, :], ones_bf[:], k2sq[:],
                                                      start=True, stop=True), pe,
                             fence=ps[4][0:1, 0:128])
                        P.wait("ve", pe)
                        P.do("ve", lambda v, sl8=sl8: v.reduce_max(
                            out=kmax8[0:1, sl8:sl8 + 1], in_=ps[4][0:1, :],
                            axis=AX.X), dve)
                    P.do("ve", lambda v: v.reduce_max(out=km2[0:1, 0:1], in_=kmax8[:],
                                                      axis=AX.X), dve)
                    P.wait("ve", dve)
                    P.do("ve", lambda v: v.tensor_add(out=km2[0:1, 1:2],
                                                      in0=km2[0:1, 0:1],
                                                      in1=kmr[0:1, 3:4]), dve)
                    P.wait("sc", dve)
                    P.do("sc", lambda s: s.activation(out=km2[0:1, 2:3], in_=km2[0:1, 1:2],
                                                      func=AF.Ln), act)
                    P.wait("ve", act)
                    P.do("ve", lambda v: v.tensor_scalar_mul(
                        out=km2[0:1, 3:4], in0=km2[0:1, 2:3], scalar1=0.5), dve)
                    for kc in range(32):
                        pb = psf[kc % 2][:, 0:128]
                        P.wait("te", act)
                        for c in range(4):
                            P.do("te", lambda t, pb=pb, c=c, kc=kc, bi=bi: t.matmul(
                                pb, kcnT[:, c, kc * 128:(kc + 1) * 128],
                                wkbv_sb[:, bi, c, :], start=(c == 0), stop=(c == 3)),
                                pe if c == 3 else None,
                                fence=(pb[0:1, 0:128] if c == 3 else None))
                        P.wait("sc", pe)
                        P.do("sc", lambda s, pb=pb, kc=kc: s.activation(
                            out=Vkm[:, kc, :], in_=pb, func=AF.Copy), act)
                    P.wait("te", act)
                    P.wait("te", lds, qw_need)
                    for qs in range(4):
                        pb = ps[qs % 2]
                        sl = slice(qs * 512, (qs + 1) * 512)
                        P.wait("te", dve)
                        for c in range(12):
                            P.do("te", lambda t, pb=pb, c=c, sl=sl: t.matmul(
                                pb[:], wqbn_sb[:, c, :], qcgT[:, c, sl],
                                start=(c == 0), stop=(c == 11)), pe if c == 11 else None,
                                fence=(pb[0:1, 0:128] if c == 11 else None))
                        P.wait("ve", pe)
                        P.do("ve", lambda v, pb=pb, sl=sl: v.tensor_mul(
                            out=qnT[:, sl], in0=pb[:], in1=rbq[:, sl]), dve)
                    for qs in range(4):
                        pb = ps[2 + qs % 2]
                        sl = slice(qs * 512, (qs + 1) * 512)
                        P.wait("sy", dve, cos_free)
                        P.dma("sy", cosO_sb[:], cosO[:, sl], ld)
                        P.dma("sy", sinO_sb[:], sinO[:, sl], ld)
                        P.wait("te", dve)
                        for c in range(12):
                            P.do("te", lambda t, pb=pb, c=c, sl=sl: t.matmul(
                                pb[0:64, :], wqbr_sb[:, c, :], qcgT[:, c, sl],
                                start=(c == 0), stop=(c == 11)), pe if c == 11 else None,
                                fence=(pb[0:1, 0:128] if c == 11 else None))
                        P.wait("ve", pe)
                        P.wait("ve", ld)
                        P.do("ve", lambda v, pb=pb: v.tensor_mul(
                            out=qrf_a[:], in0=pb[0:32, :], in1=rbq[0:32, 0:512]))
                        P.do("ve", lambda v, pb=pb: v.tensor_mul(
                            out=qrf_b[:], in0=pb[32:64, :], in1=rbq[0:32, 0:512]), dve)
                        P.wait("ve", dve)
                        P.do("ve", lambda v: v.tensor_mul(out=qha[:], in0=qrf_a[:],
                                                          in1=cosO_sb[0:32, :]))
                        P.do("ve", lambda v: v.tensor_mul(out=qhb[:], in0=qrf_b[:],
                                                          in1=sinO_sb[0:32, :]), dve)
                        P.wait("ve", dve)
                        P.do("ve", lambda v: v.tensor_sub(out=qtmp[0:32, :],
                                                          in0=qha[:], in1=qhb[:]))
                        P.do("ve", lambda v: v.tensor_mul(out=qha[:], in0=qrf_b[:],
                                                          in1=cosO_sb[0:32, :]))
                        P.do("ve", lambda v: v.tensor_mul(out=qhb[:], in0=qrf_a[:],
                                                          in1=sinO_sb[0:32, :]), dve)
                        P.wait("ve", dve)
                        P.do("ve", lambda v: v.tensor_add(out=qtmp[32:64, :],
                                                          in0=qha[:], in1=qhb[:]), dve)
                        cos_free = dve.n
                        P.wait("sc", dve)
                        P.do("sc", lambda s, sl=sl: s.activation(
                            out=qrA[0:RD, sl], in_=qtmp[:], func=AF.Copy), act)
                    for qt in range(N_QT):
                        sl = slice(qt * QT, (qt + 1) * QT)
                        P.wait("ve", act)
                        P.do("ve", lambda v, sl=sl: v.tensor_mul(
                            out=qsq_b[:], in0=qnT[:, sl], in1=qnT[:, sl]))
                        P.do("ve", lambda v, sl=sl: v.tensor_mul(
                            out=qsq_r[:], in0=qrA[0:RD, sl], in1=qrA[0:RD, sl]), dve)
                        P.wait("te", dve)
                        P.wait("te", act)
                        P.do("te", lambda t: t.matmul(
                            ps[4][0:1, :], ones64_bf[:], qsq_r[:],
                            start=True, stop=False))
                        P.do("te", lambda t: t.matmul(
                            ps[4][0:1, :], ones_bf[:], qsq_b[:],
                            start=False, stop=True), pe,
                             fence=ps[4][0:1, 0:128])
                        P.wait("sc", pe)
                        P.do("sc", lambda s: s.activation(out=crow[0:1, 0:512],
                                                          in_=ps[4][0:1, :],
                                                          func=AF.Ln,
                                                          bias=c_tiny[0:1, :]), act)
                        P.wait("sc", act)
                        P.do("sc", lambda s: s.activation(out=rdrow[:],
                                                          in_=crow[0:1, 0:512],
                                                          func=AF.Exp, scale=0.5,
                                                          bias=km2[0:1, 3:4]), act)
                        P.wait("ve", act)
                        P.do("ve", lambda v, sl=sl: v.tensor_scalar_mul(
                            out=qrA[RD:RD + 1, sl], in0=rdrow[:],
                            scalar1=-1.0), dve)
                        P.wait("te", dve)
                        exp_base = s_exp.n
                        pe_base = pe.n
                        pc = ps[2 + qt % 2]
                        for kc in range(32):
                            pb = ps[kc % 2]
                            if kc >= 2:
                                P.wait("te", s_exp, exp_base + kc - 1)
                            P.do("te", lambda t, pb=pb, kc=kc, sl=sl: t.matmul(
                                pb[:], kroT[:, kc * 128:(kc + 1) * 128], qrA[:, sl],
                                start=True, stop=False))
                            P.do("te", lambda t, pb=pb, kc=kc, sl=sl: t.matmul(
                                pb[:], KnT[:, kc * 128:(kc + 1) * 128], qnT[:, sl],
                                start=False, stop=True), pe)
                            if kc >= 1:
                                P.wait("te", s_exp, exp_base + kc)
                                P.do("te", lambda t, pc=pc, kc=kc: t.matmul(
                                    pc[:], Vkm[:, kc - 1, :], probs[:, kc - 1, :],
                                    start=(kc == 1), stop=False))
                            P.wait("sc", pe, pe_base + kc + 1)
                            P.do("sc", lambda s, pb=pb, kc=kc: s.activation(
                                out=probs[:, kc, :], in_=pb[:], func=AF.Exp), s_exp)
                            if kc >= 1:
                                P.wait("ve", s_exp, exp_base + kc)
                                if kc == 1:
                                    P.wait("ve", pe, den_guard_pe)
                                    P.wait("ve", act, den_guard_act)
                                    P.do("ve", lambda v: v.tensor_copy(
                                        out=dpart[:], in_=probs[:, 0, :]), dve)
                                else:
                                    P.wait("ve", dve)
                                    P.do("ve", lambda v, kc=kc: v.tensor_add(
                                        out=dpart[:], in0=dpart[:],
                                        in1=probs[:, kc - 1, :]), dve)
                        P.wait("te", s_exp, exp_base + 32)
                        P.do("te", lambda t, pc=pc: t.matmul(
                            pc[:], Vkm[:, 31, :], probs[:, 31, :],
                            start=False, stop=True), pe,
                             fence=pc[0:1, 0:128])
                        P.wait("ve", s_exp, exp_base + 32)
                        P.wait("ve", dve)
                        P.do("ve", lambda v: v.tensor_add(
                            out=dpart[:], in0=dpart[:], in1=probs[:, 31, :]), dve)
                        P.wait("te", dve)
                        P.do("te", lambda t: t.matmul(ps[5][0:1, :], ones_f[:],
                                                      dpart[:], start=True, stop=True),
                             pe, fence=ps[5][0:1, 0:128])
                        den_guard_pe = pe.n
                        P.wait("sc", pe)
                        P.do("sc", lambda s: s.activation(out=dpart[0:1, :],
                                                          in_=ps[5][0:1, :],
                                                          func=AF.Ln,
                                                          bias=c_tiny[0:1, :]), act)
                        P.wait("sc", act)
                        P.do("sc", lambda s: s.activation(out=rdrow[:],
                                                          in_=dpart[0:1, :],
                                                          func=AF.Exp,
                                                          scale=-1.0), act)
                        den_guard_act = act.n
                        P.wait("te", act)
                        P.do("te", lambda t: t.matmul(ps[5][:], onesr_f[:],
                                                      rdrow[0:1, :],
                                                      start=True, stop=True), pe,
                             fence=ps[5][0:1, 0:128])
                        P.wait("sc", pe)
                        P.do("sc", lambda s: s.activation(out=rbden[:], in_=ps[5][:],
                                                          func=AF.Copy), act)
                        P.wait("ve", act)
                        P.wait("ve", st, osc_guard)
                        P.do("ve", lambda v, pc=pc: v.tensor_mul(
                            out=obf[:], in0=pc[:], in1=rbden[:]), dve)
                        P.wait("sy", dve)
                        P.dma("sy", osc_d[h, qt], obf[:], st)
                        osc_guard = st.n
                    n_emit = (n_dq * (h + 1)) // NH - dq_pos
                    for _ in range(n_emit):
                        kind, a1_, a2_ = dq_units[dq_pos]
                        if kind == "gu":
                            emit_gu_unit(a1_, a2_)
                        else:
                            emit_dn_unit(a1_, a2_)
                        dq_pos += 1
                if _SBUF_DEBUG:
                    dbg_list = [("kcnT", kcnT), ("kroT", kroT), ("kmr", kmr),
                                ("qcgT", qcgT), ("rbq", rbq), ("KnT", KnT),
                                ("Vkm", Vkm), ("qnT", qnT), ("qrA", qrA),
                                ("km2", km2), ("kmax8", kmax8), ("probs", probs),
                                ("rbden", rbden), ("dpart", dpart)]
                    for nm, tns in dbg_list:
                        shp = list(tns[:].shape)
                        dt = tns[:].dtype
                        dd = nc.dram_tensor("dbg_" + nm, shp, dt,
                                            kind="ExternalOutput").ap()
                        P.wait("gp", dve)
                        P.wait("gp", act)
                        P.wait("gp", pe)
                        P.dma("gp", dd, tns[:], st)
                dq_flush()
                P.wait("gp", st)
                P.wait("gp", lds)
                P.wait("sy", st)
                P.wait("sy", ld)
                P.replay(nc)

        esAB.close()

        if "b3" in _PHASES:
            P = Prog()
            with ExitStack() as psB3:
                lb = lambda n, sh, dt: psB3.enter_context(nc.sbuf_tensor(n, sh, dt))
                wo_bf = lb("wo_bf", [128, 32, 1024], BF16)
                oT = lb("oT", [128, 32, 512], BF16)
                xres = lb("xres", [128, 2, 512], F32)
                x2o = lb("x2o", [128, 2, 512], F32)
                x2o_free = [0, 0]
                xres_free = [0, 0]
                for jg in range(4):
                    P.wait("gp", pe, pe.n)
                    wsl = w_o[:, jg * 1024:(jg + 1) * 1024] \
                        .rearrange("(c p) n -> p c n", p=128)
                    for cq in range(4):
                        P.dma("gp", wo_bf[:, cq * 8:(cq + 1) * 8, :],
                              wsl[:, cq * 8:(cq + 1) * 8, :], lds)
                    for qt in range(4):
                        P.wait("sy", pe, pe.n)
                        P.wait("gp", pe, pe.n)
                        for b in range(4):
                            i0 = b * 8
                            P.dma("sy" if b % 2 == 0 else "gp",
                                  oT[:, i0:i0 + 8, :],
                                  osc_d[i0:i0 + 8, qt]
                                  .rearrange("i p n -> p i n"),
                                  ld if b % 2 == 0 else lds)
                        P.wait("te", ld)
                        P.wait("te", lds)
                        dmy(P, ps[0][0:1, 0:1])
                        for sub in range(4):
                            trow = qt * 512 + sub * 128
                            for j2 in range(2):
                                bb = (sub * 2 + j2) % 2
                                pb = ps[bb]
                                col0 = jg * 1024 + j2 * 512
                                P.wait("te", dve)
                                for i in range(32):
                                    P.do("te", lambda t, pb=pb, i=i, sub=sub, j2=j2:
                                         t.matmul(pb[:],
                                                  oT[:, i, sub * 128:(sub + 1) * 128],
                                                  wo_bf[:, i, j2 * 512:(j2 + 1) * 512],
                                                  start=(i == 0), stop=(i == 31)),
                                         pe if i == 31 else None,
                                         fence=(pb[0:1, 0:128] if i == 31 else None))
                                P.wait("sy", dve, xres_free[j2])
                                P.dma("sy", xres[:, j2, :],
                                      x_seq[trow:trow + 128, col0:col0 + 512], ld)
                                P.wait("ve", pe)
                                P.wait("ve", ld)
                                P.wait("ve", st, x2o_free[j2])
                                P.do("ve", lambda v, pb=pb, j2=j2: v.tensor_add(
                                    out=x2o[:, j2, :], in0=pb[:], in1=xres[:, j2, :]),
                                    dve)
                                xres_free[j2] = dve.n
                                P.wait("sy", dve)
                                P.dma("sy", x2_d[trow:trow + 128, col0:col0 + 512],
                                      x2o[:, j2, :], st)
                                x2o_free[j2] = st.n
                P.wait("sy", st)
                P.replay(nc)

        if "c2" in _PHASES:
            for p2 in range(2):
                P = Prog()
                with ExitStack() as psC2:
                    lb = lambda n, sh, dt: psC2.enter_context(nc.sbuf_tensor(n, sh, dt))
                    x2a = lb(f"x2a{p2}", [128, 2, H], F32)
                    sqv = lb(f"sqv3{p2}", [128, H], BF16)
                    rsa = lb(f"rsa3{p2}", [128, 8], F32)
                    lnb = lb(f"lnb3{p2}", [128, H], BF16)
                    ln2T = lb(f"ln2T{p2}", [128, 32, C2P], BF16)
                    gub = lb(f"gub{p2}", [128, 2, 32, 256], BF16)
                    sg = lb(f"sg{p2}", [128, C2P], BF16)
                    ub = lb(f"ub{p2}", [128, C2P], BF16)
                    hpair = lb(f"hpair{p2}", [128, C2P], BF16)
                    oh32 = lb(f"oh32{p2}", [128, 32, 32], BF16)
                    zsg = lb(f"zsg{p2}", [32, 2, 256], F32)
                    s1sb = lb(f"s1sb{p2}", [32, C2P], F32)
                    P.dma("sy", oh32[:], oneh32, ld)
                    xfree = {}
                    for tc in range(C2P // 128):
                        b = tc % 2
                        row = p2 * C2P + tc * 128
                        if tc >= 2:
                            P.wait("sy", act, xfree[tc - 2])
                        P.dma("sy", x2a[:, b, :], x2_d[row:row + 128, :], ld)
                        P.wait("ve", ld)
                        P.wait("sc", ld)
                        emit_ln(P, x2a[:, b, :], sqv[:], rsa[:], lnb[:], H)
                        xfree[tc] = act.n
                        P.wait("te", act)
                        emit_transposes(P, lnb[:], ln2T[:], tc * 128)
                    P.wait("te", ld)
                    P.wait("te", dve)
                    dmy(P, ps[0][0:1, 0:1])
                    for th in range(2):
                        for c in range(32):
                            P.do("te", lambda t, th=th, c=c: t.matmul(
                                ps[th][0:32, :], oh32[:, c, :],
                                ln2T[:, c, th * 512:(th + 1) * 512],
                                start=(c == 0), stop=(c == 31)),
                                pe if c == 31 else None,
                                fence=(ps[th][0:1, 0:128] if c == 31 else None))
                    P.wait("sc", pe)
                    for th in range(2):
                        P.do("sc", lambda s, th=th: s.activation(
                            out=s1sb[0:32, th * 512:(th + 1) * 512],
                            in_=ps[th][0:32, :], func=AF.Copy), act)
                    P.wait("te", act)
                    gu_free = [0, 0]
                    ev_act = [0, 0]
                    ev_dve = [0, 0]
                    h_free = 0
                    for pair in range(NPAIR):
                        bi = pair % 2
                        P.wait("sy", pe, gu_free[bi])
                        for k in range(32):
                            P.dma("sy", gub[:, bi, k, :],
                                  wgu_d[k, :, pair * 256:(pair + 1) * 256], ld)
                        P.dma("sy", zsg[:, bi, :],
                              zsgu_d[:, pair * 256:(pair + 1) * 256], ld)
                        bs = pair % 2
                        gps = (ps[0], ps[1]) if bs == 0 else (ps[4], ps[5])
                        ups = (ps[2], ps[3]) if bs == 0 else (psf[0], psf[1])
                        P.wait("te", ld)
                        P.wait("te", act, ev_act[bs])
                        P.wait("te", dve, ev_dve[bs])
                        dmy(P, gps[0][0:1, 0:1])
                        for gu in range(2):
                            dsts = gps if gu == 0 else ups
                            for th in range(2):
                                P.do("te", lambda t, d=dsts[th], gu=gu, th=th, bi=bi:
                                     t.matmul(d[:], zsg[:, bi, gu * 128:(gu + 1) * 128],
                                              s1sb[0:32, th * 512:(th + 1) * 512],
                                              start=True, stop=False))
                                for k in range(32):
                                    if th == 0:
                                        P.do("te", lambda t, gu=gu, k=k, bi=bi:
                                             t.ldweights(
                                                 gub[:, bi, k, gu * 128:(gu + 1) * 128]))
                                    P.do("te", lambda t, d=dsts[th], gu=gu, k=k, bi=bi,
                                         th=th: t.matmul(
                                             d[:],
                                             gub[:, bi, k, gu * 128:(gu + 1) * 128],
                                             ln2T[:, k, th * 512:(th + 1) * 512],
                                             start=False, stop=(k == 31)),
                                         pe if k == 31 else None,
                                         fence=(dsts[th][0:1, 0:128]
                                                if k == 31 else None))
                        gu_free[bi] = pe.n
                        P.wait("sc", pe)
                        P.wait("sc", dve, ev_dve[bs])
                        for th in range(2):
                            P.do("sc", lambda s, d=gps[th], th=th: s.activation(
                                out=sg[:, th * 512:(th + 1) * 512], in_=d[:],
                                func=AF.Silu))
                            P.do("sc", lambda s, d=ups[th], th=th: s.activation(
                                out=ub[:, th * 512:(th + 1) * 512], in_=d[:],
                                func=AF.Copy), act)
                        ev_act[bs] = act.n
                        P.wait("ve", act)
                        P.wait("ve", st, h_free)
                        P.do("ve", lambda v: v.tensor_mul(out=hpair[:], in0=sg[:],
                                                          in1=ub[:]), dve)
                        ev_dve[bs] = dve.n
                        P.wait("gp", dve)
                        P.dma("gp", h_d[p2, pair], hpair[:], st)
                        h_free = st.n
                    P.wait("gp", st)
                    P.wait("sy", st)
                    P.replay(nc)

        if "c3" in _PHASES:
            P = Prog()
            with ExitStack() as psC3:
                lb = lambda n, sh, dt: psC3.enter_context(nc.sbuf_tensor(n, sh, dt))
                hbuf = lb("hbuf", [128, NPAIR, 512], BF16)
                dnb = lb("dnb", [128, 2, 8, 512], BF16)
                x2r = lb("x2r", [128, 4, 512], F32)
                yout = lb("yout", [128, 4, 512], F32)
                oh86 = lb("oh86", [128, NPAIR, NPAIR], BF16)
                zsdn_sb = lb("zsdn_sb", [NPAIR, H], F32)
                s2sb = lb("s2sb", [NPAIR, 512], F32)
                P.dma("sy", oh86[:], oneh86, ld)
                P.dma("sy", zsdn_sb[:], zsdn_d, ld)
                dnb_free = [0, 0]
                x2r_free = 0
                yout_free = 0
                for tt4 in range(4):
                    t0 = tt4 * 512
                    p2 = t0 // C2P
                    off = t0 % C2P
                    P.wait("sy", pe, pe.n)
                    P.wait("gp", pe, pe.n)
                    for b in range((NPAIR + 7) // 8):
                        i0 = b * 8
                        gi = min(8, NPAIR - i0)
                        P.dma("sy" if b % 2 == 0 else "gp", hbuf[:, i0:i0 + gi, :],
                              h_d[p2, i0:i0 + gi][:, :, off:off + 512]
                              .rearrange("i p n -> p i n"),
                              ld if b % 2 == 0 else lds)
                    P.wait("te", ld)
                    P.wait("te", lds)
                    P.wait("te", act)
                    dmy(P, ps[5][0:1, 0:1])
                    for i in range(NPAIR):
                        P.do("te", lambda t, i=i: t.matmul(
                            ps[5][0:NPAIR, :], oh86[:, i, :], hbuf[:, i, :],
                            start=(i == 0), stop=(i == NPAIR - 1)),
                            pe if i == NPAIR - 1 else None,
                            fence=(ps[5][0:1, 0:128] if i == NPAIR - 1 else None))
                    P.wait("sc", pe)
                    P.do("sc", lambda s: s.activation(out=s2sb[:],
                                                      in_=ps[5][0:NPAIR, :],
                                                      func=AF.Copy), act)
                    P.wait("te", act)
                    for og in range(8):
                        bs = og % 2
                        psl = (ps[0], ps[1], ps[2], ps[3]) if bs == 0 else \
                              (ps[4], ps[5], psf[0], psf[1])
                        P.wait("te", dve)
                        dmy(P, psl[0][0:1, 0:1])
                        for tc in range(4):
                            P.do("te", lambda t, d=psl[tc], tc=tc, og=og: t.matmul(
                                d[:], s2sb[:, tc * 128:(tc + 1) * 128],
                                zsdn_sb[:, og * 512:(og + 1) * 512],
                                start=True, stop=False))
                        nbat = (NPAIR + 7) // 8
                        ldmark = {}

                        def issue_dnb(b):
                            i0 = b * 8
                            gi = min(8, NPAIR - i0)
                            bb = b % 2
                            q, sem = ("sy", ld) if b % 2 == 0 else ("gp", lds)
                            P.wait(q, pe, dnb_free[bb])
                            P.dma(q, dnb[:, bb, 0:gi, :],
                                  wdn_d[i0:i0 + gi, :, og * 512:(og + 1) * 512]
                                  .rearrange("i p n -> p i n"), sem)
                            ldmark[b] = (sem, sem.n)

                        issue_dnb(0)
                        for b in range(nbat):
                            if b + 1 < nbat:
                                issue_dnb(b + 1)
                            bb = b % 2
                            P.wait("te", ldmark[b][0], ldmark[b][1])
                            for ii in range(min(8, NPAIR - b * 8)):
                                i = b * 8 + ii
                                dmy(P, psl[0][0:1, 0:1])
                                for tc in range(4):
                                    P.do("te", lambda t, i=i, tc=tc: t.ldweights(
                                        hbuf[:, i, tc * 128:(tc + 1) * 128]))
                                    P.do("te", lambda t, d=psl[tc], i=i, tc=tc,
                                         bb=bb, ii=ii:
                                         t.matmul(d[:],
                                                  hbuf[:, i, tc * 128:(tc + 1) * 128],
                                                  dnb[:, bb, ii, :],
                                                  start=False, stop=(i == NPAIR - 1)),
                                         pe if tc == 3 else None,
                                         fence=(psl[tc][0:1, 0:128]
                                                if (i == NPAIR - 1 and tc == 3)
                                                else None))
                            dnb_free[bb] = pe.n
                        P.wait("sy", dve, x2r_free)
                        P.dma("sy", x2r[:],
                              x2_d[t0:t0 + 512, og * 512:(og + 1) * 512]
                              .rearrange("(c p) n -> p c n", p=128), ld)
                        P.wait("ve", pe)
                        P.wait("ve", ld)
                        P.wait("ve", st, yout_free)
                        for tc in (3, 2, 1, 0):
                            P.do("ve", lambda v, d=psl[tc], tc=tc: v.tensor_add(
                                out=yout[:, tc, :], in0=d[:], in1=x2r[:, tc, :]),
                                dve if tc == 0 else None)
                        x2r_free = dve.n
                        P.wait("sy", dve)
                        P.dma("sy", y[t0:t0 + 512, og * 512:(og + 1) * 512]
                              .rearrange("(c p) n -> p c n", p=128),
                              yout[:], st)
                        yout_free = st.n
                P.wait("sy", st)
                P.replay(nc)

    return nc



def _host_prep(inputs):
    x = np.ascontiguousarray(inputs["L0_attn_hidden_states"])
    w_qa = np.ascontiguousarray(inputs["L0_attn_w_qa"])
    g_qa = np.ascontiguousarray(inputs["L0_attn_g_qa"])
    w_qb = np.ascontiguousarray(inputs["L0_attn_w_qb"])
    w_kva = np.ascontiguousarray(inputs["L0_attn_w_kva"])
    g_kva = np.ascontiguousarray(inputs["L0_attn_g_kva"])
    w_kvb = np.ascontiguousarray(inputs["L0_attn_w_kvb"])
    w_o = np.ascontiguousarray(inputs["L0_attn_w_o"])
    qw_gu = np.ascontiguousarray(inputs["L0_ffn_qweight_gu"])
    qz_gu = np.ascontiguousarray(inputs["L0_ffn_qzeros_gu"])
    sc_gu = np.ascontiguousarray(inputs["L0_ffn_scales_gu"])
    qw_dn = np.ascontiguousarray(inputs["L0_ffn_qweight_dn"])
    qz_dn = np.ascontiguousarray(inputs["L0_ffn_qzeros_dn"])
    sc_dn = np.ascontiguousarray(inputs["L0_ffn_scales_dn"])
    bf = ml_dtypes.bfloat16

    wqa_l = np.ascontiguousarray(
        w_qa.reshape(32, 128, 2, 768).transpose(2, 1, 0, 3))
    g_qa_l = np.ascontiguousarray(g_qa.reshape(12, 128).T)
    wq = w_qb.reshape(12, 128, NH, QK)
    wqb_n = np.ascontiguousarray(wq[:, :, :, :ND].transpose(2, 1, 0, 3))
    wqb_r = np.ascontiguousarray(wq[:, :, :, ND:].transpose(2, 1, 0, 3))
    w_kva_l = np.ascontiguousarray(w_kva.reshape(32, 128, 576).transpose(1, 0, 2))
    g_kva_l = np.ascontiguousarray(g_kva.reshape(4, 128).T)
    wkv = w_kvb.reshape(4, 128, NH, ND + VD)
    wkvb_k = np.ascontiguousarray(wkv[:, :, :, :ND].transpose(2, 1, 0, 3))
    wkvb_v = np.ascontiguousarray(wkv[:, :, :, ND:].transpose(2, 1, 0, 3))

    blk_order = np.empty(2 * NPAIR, dtype=np.int64)
    blk_order[0::2] = np.arange(NPAIR)
    blk_order[1::2] = NPAIR + np.arange(NPAIR)
    pidx = (blk_order[:, None] * 16 + np.arange(16)[None, :]).reshape(-1)
    qw_gu_p = np.ascontiguousarray(qw_gu[:, pidx])
    qz_gu_p = np.ascontiguousarray(qz_gu[:, pidx])
    scp = sc_gu.reshape(32, 172, 16, 8)[:, blk_order, :, :]
    sc_gu_p = np.ascontiguousarray(scp.transpose(0, 1, 3, 2).reshape(32, GU_COLS))
    qw_dn_p = np.ascontiguousarray(
        qw_dn.reshape(NPAIR, 16, 8, 512).transpose(0, 2, 1, 3).reshape(INTER, 512))

    inv = 1.0 / (10000.0 ** (np.arange(0, RD, 2, dtype=np.float64) / RD))

    def rope_tables(pos):
        fr = pos[None, :] * inv[:, None]
        c = np.cos(fr).astype(np.float32)
        s = np.sin(fr).astype(np.float32)
        return (np.ascontiguousarray(np.concatenate([c, c], axis=0)),
                np.ascontiguousarray(np.concatenate([s, s], axis=0)))

    ident = np.eye(128, dtype=np.float32).astype(bf)
    oneh32 = np.ascontiguousarray(
        np.broadcast_to(np.eye(32, dtype=np.float32), (128, 32, 32))).astype(bf)
    oneh86 = np.ascontiguousarray(
        np.broadcast_to(np.eye(NPAIR, dtype=np.float32), (128, NPAIR, NPAIR))).astype(bf)

    shared = dict(
        wqa_l=wqa_l, g_qa=g_qa_l, wqb_n=wqb_n, wqb_r=wqb_r,
        w_kva=w_kva_l, g_kva=g_kva_l, wkvb_k=wkvb_k, wkvb_v=wkvb_v,
        w_o=w_o, qw_gu=qw_gu_p, qz_gu=qz_gu_p, sc_gu=sc_gu_p,
        qw_dn=qw_dn_p, qz_dn=qz_dn, sc_dn=sc_dn, ident=ident,
        oneh32=oneh32, oneh86=oneh86,
    )
    in_maps = []
    for c in range(8):
        b, hf = divmod(c, 2)
        if hf == 0:
            xs = x[b]
            pos = np.arange(S, dtype=np.float64)
        else:
            xs = np.concatenate([x[b, T:], x[b, :T]], axis=0)
            pos = np.concatenate([np.arange(T, S), np.arange(0, T)]).astype(np.float64)
        cF, sF = rope_tables(pos)
        cO, sO = rope_tables(pos[:T])
        m = dict(shared)
        m.update(x_seq=np.ascontiguousarray(xs), cosF=cF, sinF=sF, cosO=cO, sinO=sO)
        in_maps.append(m)
    return in_maps


def kernel(**inputs):
    in_maps = _host_prep(inputs)
    nc = build_program()
    res = run_bass_kernel_spmd(nc, in_maps, core_ids=list(range(8)))
    out = np.empty((4, S, H), dtype=np.float32)
    for c in range(8):
        b, hf = divmod(c, 2)
        out[b, hf * T:(hf + 1) * T, :] = res.results[c]["y"]
    return out
